# version 02 (final)
# baseline (speedup 1.0000x reference)
import numpy as np
from contextlib import ExitStack

import concourse.bass as bass
import concourse.mybir as mybir
from concourse.bass_utils import run_bass_kernel_spmd

F32 = mybir.dt.float32
BF16 = mybir.dt.bfloat16
I32 = mybir.dt.int32
AF = mybir.ActivationFunctionType
ALU = mybir.AluOpType
AXX = mybir.AxisListType.X

B, S, HID = 2, 4096, 4096
H, P, N, G, KCONV = 128, 64, 128, 8, 4
INTER = H * P
TOK = 1024
NT = TOK // 128
HC, GC = 32, 2
CCOL = 4640
HBCC = 2560
NCB = HBCC // 128
NZ = 32
NSUP, ZS = 4, 8
WNB, WBW = 18, 256
EPS = 1e-5
RG = [[0, 1, 2, 3], [4, 5, 6, 7]]


class Sem:
    def __init__(self, ctx, nc, name):
        self.h = ctx.enter_context(nc.semaphore(name))
        self.v = 0


class KB:

    def __init__(self):
        self.nc = bass.Bass("TRN2", target_bir_lowering=False)
        self.ev = {}

    def T(self, fn): self.block.tensor(fn)
    def V(self, fn): self.block.vector(fn)
    def C(self, fn): self.block.scalar(fn)
    def Y(self, fn): self.block.sync(fn)
    def G(self, fn): self.block.gpsimd(fn)

    def vch(self, v, inst):
        inst.then_inc(self.vD.h, 1)
        self.vD.v += 1
        v.wait_ge(self.vD.h, self.vD.v)

    def sch(self, sc, inst):
        inst.then_inc(self.sD.h, 1)
        self.sD.v += 1
        sc.wait_ge(self.sD.h, self.sD.v)

    def mark(self, name, sem, n=1):
        sem.v += n
        self.ev[name] = (sem, sem.v)

    def w(self, eng, name):
        sem, val = self.ev[name]
        eng.wait_ge(sem.h, val)

    def build(self):
        nc = self.nc
        ap = lambda t: t.ap()
        self.x_in = ap(nc.dram_tensor("x", [TOK, HID], F32, kind="ExternalInput"))
        self.wsl = ap(nc.dram_tensor("wsl", [HID, CCOL], F32, kind="ExternalInput"))
        self.convw = ap(nc.dram_tensor("convw", [128, NCB * 4], F32, kind="ExternalInput"))
        self.convb = ap(nc.dram_tensor("convb", [128, NCB], F32, kind="ExternalInput"))
        self.dtb = ap(nc.dram_tensor("dtb", [1, HC], F32, kind="ExternalInput"))
        self.alog = ap(nc.dram_tensor("alog", [HC, 1], F32, kind="ExternalInput"))
        self.dvec = ap(nc.dram_tensor("dvec", [1, HC], F32, kind="ExternalInput"))
        self.nw_in = ap(nc.dram_tensor("nw", [128, 64], F32, kind="ExternalInput"))
        self.wo = ap(nc.dram_tensor("wo", [INTER, HID], F32, kind="ExternalInput"))
        self.fw = ap(nc.dram_tensor("fw", [HID, 136], F32, kind="ExternalInput"))
        self.bwf = ap(nc.dram_tensor("bwf", [128, HID], F32, kind="ExternalInput"))
        self.meta = ap(nc.dram_tensor("meta", [1, 2], I32, kind="ExternalInput"))
        self.y_out = ap(nc.dram_tensor("y", [TOK, HID], F32, kind="ExternalOutput"))

        self.xn1d = ap(nc.dram_tensor("xn1d", [TOK, HID], BF16, kind="Internal"))
        self.x1d = ap(nc.dram_tensor("x1d", [TOK, HID], F32, kind="Internal"))
        self.agin = nc.dram_tensor("agin", [TOK, HID], BF16, kind="Internal")
        self.agout = nc.dram_tensor("agout", [S, HID], BF16, kind="Internal")
        self.hbcd = ap(nc.dram_tensor("hbcd", [S, HBCC], BF16, kind="Internal"))
        self.hsild = ap(nc.dram_tensor("hsild", [2048, S], BF16, kind="Internal"))
        self.gated = ap(nc.dram_tensor("gated", [S, 2048], BF16, kind="Internal"))
        self.gin = nc.dram_tensor("gin", [S, 2048], BF16, kind="Internal")
        self.gout4 = nc.dram_tensor("gout4", [4 * S, 2048], BF16, kind="Internal")
        self.gq = nc.dram_tensor("gq", [4 * TOK, 2048], BF16, kind="Internal")
        self.x2d = ap(nc.dram_tensor("x2d", [TOK, HID], F32, kind="Internal"))
        self.xn3d = ap(nc.dram_tensor("xn3d", [TOK, HID], BF16, kind="Internal"))

        ctx = ExitStack()
        sb = lambda name, shape, dt=F32: ctx.enter_context(nc.sbuf_tensor(name, shape, dt))
        ps = lambda name, shape, dt=F32: ctx.enter_context(nc.psum_tensor(name, shape, dt))
        with ctx:
            self.FWs = sb("FWs", [128, 32 * 136], BF16)
            self.BWs = sb("BWs", [128, HID], BF16)
            self.WH = sb("WH", [128, 128], BF16)
            self.WHT = sb("WHT", [128, 128], BF16)
            self.mvec = sb("mvec", [128, 8])
            self.eq8 = sb("eq8", [128, 8])
            self.cum8 = sb("cum8", [128, 8])
            self.rmx = sb("rmx", [128, 1])
            self.tails = sb("tails", [128, NCB * 3], BF16)
            self.cw = sb("cw", [128, NCB * 4])
            self.cb_t = sb("cb_t", [128, NCB])
            self.identf = sb("identf", [128, 128])
            self.identb = sb("identb", [128, 128], BF16)
            self.selbank = sb("selbank", [HC, HC * 128], BF16)
            self.sel127 = sb("sel127", [128, 128])
            self.onesr = sb("onesr", [1, 128])
            self.A_col = sb("A_col", [HC, 1])
            self.dtb_s = sb("dtb_s", [1, HC])
            self.dvec_s = sb("dvec_s", [1, HC])
            self.nw_s = sb("nw_s", [128, 64])
            self.meta_s = ctx.enter_context(nc.sbuf_tensor("meta_s", [1, 2], I32))
            self.junk = sb("junk", [128, 256])
            self.Drep = sb("Drep", [128, HC])

            self.psA = ps("psA", [128, 2048])
            self.psC = ps("psC", [128, 1024])
            self.psD = ps("psD", [128, 512])
            self.psE = ps("psE", [128, 512], BF16)

            self.qG = Sem(ctx, nc, "qG")
            self.qS = Sem(ctx, nc, "qS")
            self.qC = Sem(ctx, nc, "qC")
            self.qP0 = Sem(ctx, nc, "qP0")
            self.qX0 = [Sem(ctx, nc, f"qX0_{i}") for i in range(2)]
            self.qNW = Sem(ctx, nc, "qNW")
            self.qAGW = Sem(ctx, nc, "qAGW")
            self.qX1W = Sem(ctx, nc, "qX1W")
            self.qW = [Sem(ctx, nc, f"qW_{i}") for i in range(2)]
            self.qWD = Sem(ctx, nc, "qWD")
            self.qST = [Sem(ctx, nc, f"qST_{i}") for i in range(4)]
            self.qHT = [Sem(ctx, nc, f"qHT_{i}") for i in range(2)]
            self.qCV = [Sem(ctx, nc, f"qCV_{i}") for i in range(2)]
            self.qHS = Sem(ctx, nc, "qHS")
            self.qGT = Sem(ctx, nc, "qGT")
            self.qGW = Sem(ctx, nc, "qGW")
            self.qGQ = Sem(ctx, nc, "qGQ")
            self.qGO = Sem(ctx, nc, "qGO")
            self.qWO = [Sem(ctx, nc, f"qWO_{i}") for i in range(2)]
            self.qXS = [Sem(ctx, nc, f"qXS_{i}") for i in range(2)]
            self.qOW = [Sem(ctx, nc, f"qOW_{i}") for i in range(2)]
            self.qY = Sem(ctx, nc, "qY")
            self.tD = Sem(ctx, nc, "tD")
            self.vD = Sem(ctx, nc, "vD")
            self.sD = Sem(ctx, nc, "sD")
            self.gD = Sem(ctx, nc, "gD")
            self.cc = Sem(ctx, nc, "cc")

            with nc.Block() as block:
                self.block = block
                self.p0()
                with ExitStack() as c1:
                    sb1 = lambda name, shape, dt=F32: c1.enter_context(nc.sbuf_tensor(name, shape, dt))
                    self.X0 = [sb1(f"X0_{i}", [128, HID]) for i in range(2)]
                    self.XW = sb1("XW", [128, HID])
                    self.XNB = sb1("XNB", [128, HID], BF16)
                    self.xnT = sb1("xnT", [128, 32 * TOK], BF16)
                    self.p1()
                with ExitStack() as c2:
                    sb2 = lambda name, shape, dt=F32: c2.enter_context(nc.sbuf_tensor(name, shape, dt))
                    self.xnT = sb2("xnT2", [128, 32 * TOK], BF16)
                    self.wload = [sb2(f"wload{i}", [128, 32 * WBW], BF16) for i in range(2)]
                    self.wdt = sb2("wdt", [128, 32 * 32], BF16)
                    self.stg = [sb2(f"stg{i}", [128, WBW], BF16) for i in range(4)]
                    self.dt_sb = sb2("dt_sb", [128, ZS * HC])
                    self.hbcT = [sb2(f"hbcT{i}", [128, 16 + 1024], BF16) for i in range(2)]
                    self.cacc = [sb2(f"cacc{i}", [128, 1024], BF16) for i in range(2)]
                    self.BC_sb = sb2("BC_sb", [128, 4 * 1024], BF16)
                    self.hs_tok = sb2("hs_tok", [128, 2048], BF16)
                    self.hsdd = sb2("hsdd", [128, 2048], BF16)
                    self.Btok = sb2("Btok", [128, 2 * 128], BF16)
                    self.dl = sb2("dl", [128, 64])
                    self.dtcT = sb2("dtcT", [HC, 128])
                    self.lndtT = sb2("lndtT", [HC, 128])
                    self.A_dt = sb2("A_dt", [HC, 128])
                    self.acs = sb2("acs", [HC, 128])
                    self.achl = sb2("achl", [HC, 256], BF16)
                    self.pk3 = sb2("pk3", [96, 128])
                    self.packT = sb2("packT", [128, 96])
                    self.ddTb = sb2("ddTb", [128, HC], BF16)
                    self.cdrep = sb2("cdrep", [128, HC])
                    self.selt = sb2("selt", [128, 2048])
                    self.expsel = sb2("expsel", [128, 2048], BF16)
                    self.prev = sb2("prev", [128, 2048])
                    self.prevbf = sb2("prevbf", [128, 2048], BF16)
                    self.Ysb = sb2("Ysb", [128, 2048])
                    self.yoffs = sb2("yoffs", [128, 1024])
                    self.hsD = sb2("hsD", [128, 1024])
                    self.gatet = sb2("gatet", [128, 2048], BF16)
                    self.gsil = sb2("gsil", [128, 2048], BF16)
                    self.ssq = sb2("ssq", [128, 8])
                    self.p2()
                with ExitStack() as c3:
                    sb3 = lambda name, shape, dt=F32: c3.enter_context(nc.sbuf_tensor(name, shape, dt))
                    self.goT = sb3("goT", [128, 32 * TOK], BF16)
                    self.wob = [sb3(f"wob{i}", [128, 32 * 256], BF16) for i in range(2)]
                    self.x1s = [sb3(f"x1s{i}", [128, 256]) for i in range(2)]
                    self.ost = [sb3(f"ost{i}", [128, 256]) for i in range(2)]
                    self.p3a()
                with ExitStack() as c4:
                    sb4 = lambda name, shape, dt=F32: c4.enter_context(nc.sbuf_tensor(name, shape, dt))
                    self.X0 = [sb4(f"X0b_{i}", [128, HID]) for i in range(2)]
                    self.XW = sb4("XWb", [128, HID])
                    self.XNB = sb4("XNBb", [128, HID], BF16)
                    self.xnT = sb4("xnT3", [128, 32 * TOK], BF16)
                    self.p3bc()
        return nc
    def p0(self):
        s = self

        def g0(g):
            g.dma_start(s.FWs[:, :].rearrange("p (c e) -> p c e", c=32),
                        s.fw.rearrange("(c p) e -> p c e", p=128)).then_inc(s.qP0.h, 16)
            g.dma_start(s.BWs[:, :], s.bwf).then_inc(s.qP0.h, 16)
            g.dma_start(s.cw[:, :], s.convw).then_inc(s.qP0.h, 16)
            g.dma_start(s.cb_t[:, :], s.convb).then_inc(s.qP0.h, 16)
            g.dma_start(s.dtb_s[:, :], s.dtb).then_inc(s.qP0.h, 16)
            g.dma_start(s.A_col[:, :], s.alog).then_inc(s.qP0.h, 16)
            g.dma_start(s.dvec_s[:, :], s.dvec).then_inc(s.qP0.h, 16)
            g.dma_start(s.nw_s[:, :], s.nw_in).then_inc(s.qP0.h, 16)
            g.dma_start(s.meta_s[:, :], s.meta).then_inc(s.qP0.h, 16)
            s.qP0.v += 16 * 9
            def step(inst):
                inst.then_inc(s.gD.h, 1)
                s.gD.v += 1
                g.wait_ge(s.gD.h, s.gD.v)
            step(g.iota(s.identf[:, :], pattern=[[1, 128]], base=0, channel_multiplier=-1,
                        allow_small_or_imprecise_dtypes=True))
            step(g.tensor_scalar(s.identf[:, :], s.identf[:, :], 0.0, None, ALU.is_equal))
            g.tensor_copy(s.identb[:, :], s.identf[:, :])
            step(g.iota(s.junk[0:HC, 0:128], pattern=[[0, 128]], base=0, channel_multiplier=1,
                        allow_small_or_imprecise_dtypes=True))
            for hh in range(HC):
                g.tensor_scalar(s.selbank[:, hh * 128:(hh + 1) * 128],
                                s.junk[0:HC, 0:128], float(hh), None, ALU.is_equal)
            step(g.iota(s.junk[:, 128:256], pattern=[[0, 128]], base=-127, channel_multiplier=1,
                        allow_small_or_imprecise_dtypes=True))
            g.tensor_scalar(s.sel127[:, :], s.junk[:, 128:256], 0.0, None, ALU.is_equal)
            g.memset(s.onesr[:, :], 1.0)
            g.memset(s.tails[:, :], 0.0).then_inc(s.gD.h, 1)
            s.gD.v += 1
            s.fillreg = g.to_reg(-80.0)
        s.G(g0)
        s.ev["p0_g"] = (s.gD, s.gD.v)
        s.ev["p0_loads"] = (s.qP0, s.qP0.v)

        def sc0(sc):
            s.w(sc, "p0_loads")
            sc.activation(s.A_col[:, :], s.A_col[:, :], AF.Exp).then_inc(s.sD.h, 1)
            s.sD.v += 1
        s.C(sc0)
        s.ev["p0_aexp"] = (s.sD, s.sD.v)

        def v0(v):
            s.w(v, "p0_aexp")
            v.tensor_scalar(s.A_col[:, :], s.A_col[:, :], -1.0, None, ALU.mult).then_inc(s.vD.h, 1)
            s.vD.v += 1
        s.V(v0)
        s.ev["p0_acol"] = (s.vD, s.vD.v)

        def t0(t):
            s.w(t, "p0_loads")
            s.w(t, "p0_g")
            t.matmul(s.psD[:, 0:HC], s.onesr[:, :], s.dvec_s[:, :],
                     start=True, stop=True).then_inc(s.tD.h, 1)
            s.tD.v += 1
        s.T(t0)
        s.ev["p0_drepmm"] = (s.tD, s.tD.v)

        def v1(v):
            s.w(v, "p0_drepmm")
            v.tensor_copy(s.Drep[:, :], s.psD[:, 0:HC]).then_inc(s.vD.h, 1)
            s.vD.v += 1
        s.V(v1)
        s.ev["p0_drep"] = (s.vD, s.vD.v)
        s.ev["psD_free"] = (s.vD, s.vD.v)

    def ln(self, tag, src, out_bf, in_ev, extra_wait=None):
        s = self

        def c1(sc):
            s.w(sc, in_ev)
            if extra_wait:
                s.w(sc, extra_wait)
            s.sch(sc, sc.activation(s.XNB[:, :], src[:, :], AF.Identity,
                                     accum_out=s.mvec[:, 0:1]))
            sc.activation(s.XNB[:, :], src[:, :], AF.Square,
                          accum_out=s.mvec[:, 1:2]).then_inc(s.sD.h, 1)
            s.sD.v += 1
        s.C(c1)
        s.ev[f"{tag}_sums"] = (s.sD, s.sD.v)

        def v1(v):
            s.w(v, f"{tag}_sums")
            s.vch(v, v.tensor_scalar(s.mvec[:, 2:3], s.mvec[:, 0:1], 1.0 / HID, None, ALU.mult))
            s.vch(v, v.tensor_scalar(s.mvec[:, 3:4], s.mvec[:, 1:2], 1.0 / HID, None, ALU.mult))
            s.vch(v, v.tensor_tensor(s.mvec[:, 4:5], s.mvec[:, 2:3], s.mvec[:, 2:3], ALU.mult))
            s.vch(v, v.tensor_tensor(s.mvec[:, 3:4], s.mvec[:, 3:4], s.mvec[:, 4:5], ALU.subtract))
            v.tensor_scalar(s.mvec[:, 3:4], s.mvec[:, 3:4], EPS, None, ALU.add).then_inc(s.vD.h, 1)
            s.vD.v += 1
        s.V(v1)
        s.ev[f"{tag}_var"] = (s.vD, s.vD.v)

        def c2(sc):
            s.w(sc, f"{tag}_var")
            sc.activation(s.mvec[:, 4:5], s.mvec[:, 3:4], AF.Sqrt).then_inc(s.sD.h, 1)
            s.sD.v += 1
        s.C(c2)
        s.ev[f"{tag}_sd"] = (s.sD, s.sD.v)

        def v2(v):
            s.w(v, f"{tag}_sd")
            s.vch(v, v.reciprocal(s.mvec[:, 5:6], s.mvec[:, 4:5]))
            s.vch(v, v.tensor_tensor(s.mvec[:, 6:7], s.mvec[:, 2:3], s.mvec[:, 5:6], ALU.mult))
            v.tensor_scalar(s.mvec[:, 6:7], s.mvec[:, 6:7], -1.0, None, ALU.mult).then_inc(s.vD.h, 1)
            s.vD.v += 1
        s.V(v2)
        s.ev[f"{tag}_rs"] = (s.vD, s.vD.v)

        def c3(sc):
            s.w(sc, f"{tag}_rs")
            sc.activation(out_bf[:, :], src[:, :], AF.Identity,
                          bias=s.mvec[:, 6:7], scale=s.mvec[:, 5:6]).then_inc(s.sD.h, 1)
            s.sD.v += 1
        s.C(c3)
        s.ev[f"{tag}_out"] = (s.sD, s.sD.v)

    def ffn_tile(self, tag, t_i, xbuf, xbuf_ev, xnT_ev, prev_tag=None, xw_free_ev=None):
        s = self

        def tmm(t):
            s.w(t, xnT_ev)
            if prev_tag is not None:
                s.w(t, f"{prev_tag}_wh")
            for k in range(32):
                inst = t.matmul(s.psA[:, 0:136],
                                s.xnT[:, k * TOK + t_i * 128: k * TOK + (t_i + 1) * 128],
                                s.FWs[:, k * 136:(k + 1) * 136],
                                start=(k == 0), stop=(k == 31))
            inst.then_inc(s.tD.h, 1)
            s.tD.v += 1
        s.T(tmm)
        s.ev[f"{tag}_hmm"] = (s.tD, s.tD.v)

        def vwh(v):
            s.w(v, f"{tag}_hmm")
            s.vch(v, v.reduce_max(s.rmx[:, :], s.psA[:, 128:136], axis=AXX))
            s.vch(v, v.tensor_scalar(s.eq8[:, :], s.psA[:, 128:136], s.rmx[:, 0:1], None, ALU.is_ge))
            s.vch(v, v.tensor_tensor_scan(s.cum8[:, :], s.eq8[:, :], s.eq8[:, :], 0.0, ALU.add, ALU.bypass))
            s.vch(v, v.scalar_tensor_tensor(s.eq8[:, :], s.cum8[:, :], 1.5, s.eq8[:, :], ALU.is_le, ALU.mult))
            v.scalar_tensor_tensor(
                s.WH[:, :].rearrange("p (e r) -> p e r", e=8),
                s.psA[:, 0:128].rearrange("p (e r) -> p e r", e=8),
                s.rmx[:, 0:1],
                s.eq8[:, :].unsqueeze(2).broadcast_to((128, 8, 16)),
                ALU.mult, ALU.mult).then_inc(s.vD.h, 1)
            s.vD.v += 1
        s.V(vwh)
        s.ev[f"{tag}_wh"] = (s.vD, s.vD.v)

        def twt(t):
            s.w(t, f"{tag}_wh")
            if prev_tag is not None:
                s.w(t, f"{prev_tag}_whtev")
            t.transpose(s.psE[:, 0:128], s.WH[:, :], s.identb[:, :]).then_inc(s.tD.h, 1)
            s.tD.v += 1
        s.T(twt)
        s.ev[f"{tag}_whT"] = (s.tD, s.tD.v)

        def cwt(sc):
            s.w(sc, f"{tag}_whT")
            sc.copy(s.WHT[:, :], s.psE[:, 0:128]).then_inc(s.sD.h, 1)
            s.sD.v += 1
        s.C(cwt)
        s.ev[f"{tag}_whtev"] = (s.sD, s.sD.v)

        for nb in range(8):
            reg = 512 + (nb % 2) * 512

            def tob(t, nb=nb, reg=reg):
                s.w(t, f"{tag}_whtev")
                if nb >= 2:
                    s.w(t, f"{tag}_res{nb-2}")
                elif prev_tag is not None:
                    s.w(t, f"{prev_tag}_res{nb+6}")
                t.matmul(s.psA[:, reg:reg + 512], s.WHT[:, :],
                         s.BWs[:, nb * 512:(nb + 1) * 512],
                         start=True, stop=True).then_inc(s.tD.h, 1)
                s.tD.v += 1
            s.T(tob)
            s.ev[f"{tag}_omm{nb}"] = (s.tD, s.tD.v)

            def vres(v, nb=nb, reg=reg):
                s.w(v, f"{tag}_omm{nb}")
                s.w(v, xbuf_ev)
                if nb == 0 and xw_free_ev is not None:
                    for e in (xw_free_ev if isinstance(xw_free_ev, (list, tuple)) else [xw_free_ev]):
                        s.w(v, e)
                v.scalar_tensor_tensor(s.XW[:, nb * 512:(nb + 1) * 512],
                                       s.psA[:, reg:reg + 512], 0.5,
                                       xbuf[:, nb * 512:(nb + 1) * 512],
                                       ALU.mult, ALU.add).then_inc(s.vD.h, 1)
                s.vD.v += 1
            s.V(vres)
            s.ev[f"{tag}_res{nb}"] = (s.vD, s.vD.v)
        s.ev[f"{tag}_res"] = s.ev[f"{tag}_res7"]

    def p1(self):
        s = self
        for t_i in range(NT):
            buf = s.X0[t_i % 2]

            def yld(y, t_i=t_i, buf=buf):
                if t_i >= 2:
                    s.w(y, f"p1a_{t_i-2}_out")
                q = s.qX0[t_i % 2]
                y.dma_start(buf[:, :], s.x_in[t_i * 128:(t_i + 1) * 128, :]).then_inc(q.h, 16)
                q.v += 16
            s.Y(yld)
            s.ev[f"p1a_{t_i}_in"] = (s.qX0[t_i % 2], s.qX0[t_i % 2].v)
            extra = f"p1a_{t_i-1}_wr" if t_i >= 1 else None
            s.ln(f"p1a_{t_i}", buf, s.XNB, f"p1a_{t_i}_in", extra_wait=extra)

            def ywr(y, t_i=t_i):
                s.w(y, f"p1a_{t_i}_out")
                y.dma_start(s.xn1d[t_i * 128:(t_i + 1) * 128, :], s.XNB[:, :]).then_inc(s.qNW.h, 16)
                s.qNW.v += 16
            s.Y(ywr)
            s.ev[f"p1a_{t_i}_wr"] = (s.qNW, s.qNW.v)

        def ctr(sc):
            s.w(sc, f"p1a_{NT-1}_wr")
            for cb in range(32):
                sc.dma_start(s.xnT[:, cb * TOK:(cb + 1) * TOK],
                             s.xn1d[:, cb * 128:(cb + 1) * 128],
                             transpose=True).then_inc(s.qC.h, 16)
                s.qC.v += 16
        s.C(ctr)
        s.ev["p1b_xnT"] = (s.qC, s.qC.v)

        for t_i in range(NT):
            buf = s.X0[t_i % 2]

            def yld(y, t_i=t_i, buf=buf):
                if t_i >= 2:
                    s.w(y, f"p1b_{t_i-2}_res")
                q = s.qX0[t_i % 2]
                y.dma_start(buf[:, :], s.x_in[t_i * 128:(t_i + 1) * 128, :]).then_inc(q.h, 16)
                q.v += 16
            s.Y(yld)
            s.ev[f"p1b_{t_i}_xin"] = (s.qX0[t_i % 2], s.qX0[t_i % 2].v)

            prev = f"p1b_{t_i-1}" if t_i >= 1 else None
            xwfree = [f"p1b_{t_i-1}_ln_out", f"p1b_{t_i-1}_x1wr"] if t_i >= 1 else None
            s.ffn_tile(f"p1b_{t_i}", t_i, buf, f"p1b_{t_i}_xin", "p1b_xnT",
                       prev_tag=prev, xw_free_ev=xwfree)

            def ywx(y, t_i=t_i):
                s.w(y, f"p1b_{t_i}_res")
                y.dma_start(s.x1d[t_i * 128:(t_i + 1) * 128, :], s.XW[:, :]).then_inc(s.qX1W.h, 16)
                s.qX1W.v += 16
            s.Y(ywx)
            s.ev[f"p1b_{t_i}_x1wr"] = (s.qX1W, s.qX1W.v)

            extra = f"p1b_{t_i-1}_agwr" if t_i >= 1 else None
            s.ln(f"p1b_{t_i}_ln", s.XW, s.XNB, f"p1b_{t_i}_res", extra_wait=extra)

            def yag(y, t_i=t_i):
                s.w(y, f"p1b_{t_i}_ln_out")
                y.dma_start(s.agin.ap()[t_i * 128:(t_i + 1) * 128, :],
                            s.XNB[:, :]).then_inc(s.qAGW.h, 16)
                s.qAGW.v += 16
            s.Y(yag)
            s.ev[f"p1b_{t_i}_agwr"] = (s.qAGW, s.qAGW.v)
            s.ev[f"p1b_{t_i}_lnw"] = s.ev[f"p1b_{t_i}_ln_out"]

        def gag(g):
            sem, val = s.ev[f"p1b_{NT-1}_agwr"]
            g.wait_ge(sem.h, val)
            g.collective_compute("AllGather", ALU.bypass, replica_groups=RG,
                                 ins=[s.agin.ap()], outs=[s.agout.ap()]).then_inc(s.cc.h, 1)
            s.cc.v += 1
        s.G(gag)
        s.ev["ag1"] = (s.cc, s.cc.v)
        s.ev["p1_x1_all"] = (s.qX1W, s.qX1W.v)
        s.ev["p1_nw_all"] = (s.qNW, s.qNW.v)

    def p2(self):
        s = self

        def gms(g):
            s.w(g, "ag1")
            s.w(g, "p1_x1_all")
            s.w(g, "p1_nw_all")
            g.memset(s.prev[:, :], 0.0)
            g.memset(s.prevbf[:, :], 0.0).then_inc(s.gD.h, 1)
            s.gD.v += 1
        s.G(gms)
        s.ev["p2_pz"] = (s.gD, s.gD.v)

        for Si in range(NSUP):
            def cxn(sc, Si=Si):
                s.w(sc, "ag1")
                s.w(sc, "p1_x1_all")
                s.w(sc, "p1_nw_all")
                if Si > 0:
                    s.w(sc, f"p2_{Si-1}_ip_done")
                for cb in range(32):
                    sc.dma_start(
                        s.xnT[:, cb * 1024:(cb + 1) * 1024],
                        s.agout.ap()[Si * 1024:(Si + 1) * 1024, cb * 128:(cb + 1) * 128],
                        transpose=True).then_inc(s.qC.h, 16)
                    s.qC.v += 16
            s.C(cxn)
            s.ev[f"p2_{Si}_xnT"] = (s.qC, s.qC.v)

            for nb in range(WNB + 1):
                wbuf = s.wdt if nb == WNB else s.wload[nb % 2]
                wwidth = 32 if nb == WNB else WBW

                def gw(g, Si=Si, nb=nb, wbuf=wbuf, wwidth=wwidth):
                    if Si == 0 and nb < 2:
                        s.w(g, "ag1")
                        s.w(g, "p1_x1_all")
                        s.w(g, "p1_nw_all")
                    if Si == 0 and nb == WNB:
                        s.w(g, "ag1")
                        s.w(g, "p1_x1_all")
                        s.w(g, "p1_nw_all")
                    if nb >= 2 and nb != WNB:
                        s.w(g, f"p2_{Si}_ipz{nb-2}_7")
                    elif nb < 2 and Si > 0:
                        s.w(g, f"p2_{Si-1}_ipz{16+nb}_7")
                    if nb == WNB and Si > 0:
                        s.w(g, f"p2_{Si-1}_ipz{WNB}_7")
                    q = s.qWD if nb == WNB else s.qW[nb % 2]
                    g.dma_start(
                        wbuf[:, 0:32 * wwidth].rearrange("p (c w) -> p c w", c=32),
                        s.wsl[:, nb * WBW:nb * WBW + wwidth].rearrange("(c p) w -> p c w", p=128),
                    ).then_inc(q.h, 16)
                    q.v += 16
                s.G(gw)
                q2 = s.qWD if nb == WNB else s.qW[nb % 2]
                s.ev[f"p2_{Si}_w{nb}"] = (q2, q2.v)

                for z in range(ZS):
                    reg = (z % 2) * 512

                    def tip(t, Si=Si, nb=nb, z=z, wbuf=wbuf, wwidth=wwidth, reg=reg):
                        if nb == 0 and z == 0:
                            s.w(t, f"p2_{Si}_xnT")
                        s.w(t, f"p2_{Si}_w{nb}")
                        idx = nb * ZS + z
                        if idx >= 2 or Si > 0:
                            pnb, pz = divmod(idx - 2, ZS) if idx >= 2 else (WNB, ZS - 2 + z)
                            key = (f"p2_{Si}_ipev{pnb}_{pz}" if idx >= 2
                                   else f"p2_{Si-1}_ipev{WNB}_{6 + z}")
                            s.w(t, key)
                        for k in range(32):
                            inst = t.matmul(s.psC[:, reg:reg + wwidth],
                                            s.xnT[:, k * 1024 + z * 128: k * 1024 + (z + 1) * 128],
                                            wbuf[:, k * wwidth:(k + 1) * wwidth],
                                            start=(k == 0), stop=(k == 31 and nb != WNB))
                        if nb == WNB:
                            inst = t.matmul(s.psC[:, reg:reg + 32], s.onesr[:, :], s.dtb_s[:, :],
                                            start=False, stop=True)
                        inst.then_inc(s.tD.h, 1)
                        s.tD.v += 1
                    s.T(tip)
                    s.ev[f"p2_{Si}_ipz{nb}_{z}"] = (s.tD, s.tD.v)

                    def cev(sc, Si=Si, nb=nb, z=z, wwidth=wwidth, reg=reg):
                        s.w(sc, f"p2_{Si}_ipz{nb}_{z}")
                        idx = nb * ZS + z
                        if nb == WNB:
                            if Si > 0:
                                s.w(sc, f"p2_{Si-1}_z{z}_dt2")
                            sc.copy(s.dt_sb[:, z * HC:(z + 1) * HC],
                                    s.psC[:, reg:reg + 32]).then_inc(s.sD.h, 1)
                            s.sD.v += 1
                        else:
                            if idx >= 4 or Si > 0:
                                key = (f"p2_{Si}_ipwr{(idx-4)//ZS}_{(idx-4)%ZS}" if idx >= 4
                                       else f"p2_{Si-1}_ipwr{(idx + WNB*ZS - 4)//ZS}_{(idx + WNB*ZS - 4)%ZS}")
                                if key in s.ev:
                                    s.w(sc, key)
                            sc.copy(s.stg[idx % 4][:, :],
                                    s.psC[:, reg:reg + wwidth]).then_inc(s.sD.h, 1)
                            s.sD.v += 1
                    s.C(cev)
                    s.ev[f"p2_{Si}_ipev{nb}_{z}"] = (s.sD, s.sD.v)

                    if nb != WNB:
                        def ywr(y, Si=Si, nb=nb, z=z):
                            s.w(y, f"p2_{Si}_ipev{nb}_{z}")
                            idx = nb * ZS + z
                            row0 = Si * 1024 + z * 128
                            col = nb * WBW
                            if col < 2048:
                                dst = s.gated[row0:row0 + 128, col:col + WBW]
                            else:
                                dst = s.hbcd[row0:row0 + 128, col - 2048:col - 2048 + WBW]
                            q = s.qST[idx % 4]
                            y.dma_start(dst, s.stg[idx % 4][:, :]).then_inc(q.h, 16)
                            q.v += 16
                        s.Y(ywr)
                        q3 = s.qST[(nb * ZS + z) % 4]
                        s.ev[f"p2_{Si}_ipwr{nb}_{z}"] = (q3, q3.v)
            s.ev[f"p2_{Si}_ip_done"] = s.ev[f"p2_{Si}_ipz{WNB}_{ZS-1}"]
            for j in range(4):
                s.ev[f"p2_{Si}_ipwr_done{j}"] = (s.qST[j], s.qST[j].v)

            for cb in range(NCB):
                def ytr(y, Si=Si, cb=cb):
                    for j in range(4):
                        s.w(y, f"p2_{Si}_ipwr_done{j}")
                    if cb >= 2 or Si > 0:
                        s.w(y, f"p2_{Si}_cv{cb-2}_done" if cb >= 2
                            else f"p2_{Si-1}_cv{NCB-2+cb}_done")
                    q = s.qHT[cb % 2]
                    y.dma_start(
                        s.hbcT[cb % 2][:, 16:16 + 1024],
                        s.hbcd[Si * 1024:(Si + 1) * 1024, cb * 128:(cb + 1) * 128],
                        transpose=True).then_inc(q.h, 16)
                    q.v += 16
                s.Y(ytr)
                s.ev[f"p2_{Si}_cvt{cb}"] = (s.qHT[cb % 2], s.qHT[cb % 2].v)

                def vcv(v, Si=Si, cb=cb):
                    s.w(v, f"p2_{Si}_cvt{cb}")
                    if cb >= 2 or Si > 0:
                        s.w(v, f"p2_{Si}_cvs{cb-2}" if cb >= 2 else f"p2_{Si-1}_cvs{NCB-2+cb}")
                    hb = s.hbcT[cb % 2]
                    ca = s.cacc[cb % 2]
                    s.vch(v, v.tensor_copy(hb[:, 13:16], s.tails[:, cb * 3:(cb + 1) * 3]))
                    s.vch(v, v.tensor_scalar(ca[:, :], hb[:, 13:13 + 1024],
                                             s.cw[:, cb * 4:cb * 4 + 1], s.cb_t[:, cb:cb + 1],
                                             ALU.mult, ALU.add))
                    for j in (1, 2, 3):
                        s.vch(v, v.scalar_tensor_tensor(ca[:, :], hb[:, 13 + j:13 + j + 1024],
                                                        s.cw[:, cb * 4 + j:cb * 4 + j + 1],
                                                        ca[:, :], ALU.mult, ALU.add))
                    v.tensor_copy(s.tails[:, cb * 3:(cb + 1) * 3],
                                  hb[:, 1037:1040]).then_inc(s.vD.h, 1)
                    s.vD.v += 1
                s.V(vcv)
                s.ev[f"p2_{Si}_cv{cb}_done"] = (s.vD, s.vD.v)

                def csl(sc, Si=Si, cb=cb):
                    s.w(sc, f"p2_{Si}_cv{cb}_done")
                    sc.activation(s.cacc[cb % 2][:, :], s.cacc[cb % 2][:, :],
                                  AF.Silu).then_inc(s.sD.h, 1)
                    s.sD.v += 1
                s.C(csl)
                s.ev[f"p2_{Si}_sil{cb}"] = (s.sD, s.sD.v)

                if cb < 16:
                    def ysw(y, Si=Si, cb=cb):
                        s.w(y, f"p2_{Si}_sil{cb}")
                        q = s.qCV[cb % 2]
                        y.dma_start(
                            s.hsild[cb * 128:(cb + 1) * 128, Si * 1024:(Si + 1) * 1024],
                            s.cacc[cb % 2][:, :]).then_inc(q.h, 16)
                        q.v += 16
                    s.Y(ysw)
                    s.ev[f"p2_{Si}_cvs{cb}"] = (s.qCV[cb % 2], s.qCV[cb % 2].v)
                else:
                    def vbc(v, Si=Si, cb=cb):
                        s.w(v, f"p2_{Si}_sil{cb}")
                        if Si > 0:
                            s.w(v, f"p2_{Si-1}_ssd_done")
                        v.tensor_copy(s.BC_sb[:, (cb - 16) * 1024:(cb - 15) * 1024],
                                      s.cacc[cb % 2][:, :]).then_inc(s.vD.h, 1)
                        s.vD.v += 1
                    s.V(vbc)
                    s.ev[f"p2_{Si}_cvs{cb}"] = (s.vD, s.vD.v)
            s.ev[f"p2_{Si}_hs_done0"] = (s.qCV[0], s.qCV[0].v)
            s.ev[f"p2_{Si}_hs_done1"] = (s.qCV[1], s.qCV[1].v)
            s.ev[f"p2_{Si}_bc_done"] = (s.vD, s.vD.v)

            for z in range(ZS):
                self.ssd_chunk(Si, z)
            s.ev[f"p2_{Si}_ssd_done"] = s.ev[f"p2_{Si}_z{ZS-1}_gw"]

        def gag2(g):
            sem, val = s.ev[f"p2_{NSUP-1}_ssd_done"]
            g.wait_ge(sem.h, val)
            g.collective_compute("AllGather", ALU.bypass, replica_groups=RG,
                                 ins=[s.gin.ap()], outs=[s.gout4.ap()]).then_inc(s.cc.h, 1)
            s.cc.v += 1
        s.G(gag2)
        s.ev["ag2"] = (s.cc, s.cc.v)

    def ssd_chunk(self, Si, z):
        s = self
        zz = Si * ZS + z
        tag = f"p2_{Si}_z{z}"

        def yhs(y):
            s.w(y, f"p2_{Si}_hs_done0")
            s.w(y, f"p2_{Si}_hs_done1")
            if zz > 0:
                s.w(y, f"p2_{Si if z>0 else Si-1}_z{z-1 if z>0 else ZS-1}_hsfree")
            y.dma_start(s.hs_tok[:, :],
                        s.hsild[0:2048, zz * 128:(zz + 1) * 128],
                        transpose=True).then_inc(s.qHS.h, 16)
            s.qHS.v += 16
        s.Y(yhs)
        s.ev[f"{tag}_hs"] = (s.qHS, s.qHS.v)

        def ygt(y):
            if zz >= 1:
                pz = zz - 1
                s.w(y, f"p2_{pz//ZS}_z{pz%ZS}_gw")
            for j in range(4):
                s.w(y, f"p2_{Si}_ipwr_done{j}")
            y.dma_start(s.gatet[:, :],
                        s.gated[zz * 128:(zz + 1) * 128, :]).then_inc(s.qGT.h, 16)
            s.qGT.v += 16
        s.Y(ygt)
        s.ev[f"{tag}_gt"] = (s.qGT, s.qGT.v)

        def cdt1(sc):
            s.w(sc, f"p2_{Si}_ipev{WNB}_{z}")
            if zz > 0:
                s.w(sc, f"p2_{Si if z>0 else Si-1}_z{z-1 if z>0 else ZS-1}_dlfree")
            dts = s.dt_sb[:, z * HC:(z + 1) * HC]
            s.sch(sc, sc.activation(s.dl[:, 0:32], dts, AF.Abs))
            s.sch(sc, sc.activation(s.dl[:, 32:64], s.dl[:, 0:32], AF.Exp, scale=-1.0))
            sc.activation(s.dl[:, 32:64], s.dl[:, 32:64], AF.Ln, bias=1.0).then_inc(s.sD.h, 1)
            s.sD.v += 1
        s.C(cdt1)
        s.ev[f"{tag}_dt1"] = (s.sD, s.sD.v)

        def vdt(v):
            s.w(v, f"{tag}_dt1")
            dts = s.dt_sb[:, z * HC:(z + 1) * HC]
            s.vch(v, v.tensor_relu(s.junk[:, 0:32], dts))
            s.vch(v, v.tensor_tensor(s.dl[:, 0:32], s.junk[:, 0:32], s.dl[:, 32:64], ALU.add))
            s.vch(v, v.tensor_scalar(s.dl[:, 0:32], s.dl[:, 0:32], 100.0, None, ALU.min))
            v.tensor_scalar(s.junk[:, 0:32], s.dl[:, 0:32], 1e-30, None, ALU.max).then_inc(s.vD.h, 1)
            s.vD.v += 1
        s.V(vdt)
        s.ev[f"{tag}_dt2"] = (s.vD, s.vD.v)

        def cdt2(sc):
            s.w(sc, f"{tag}_dt2")
            sc.activation(s.dl[:, 32:64], s.junk[:, 0:32], AF.Ln).then_inc(s.sD.h, 1)
            s.sD.v += 1
        s.C(cdt2)
        s.ev[f"{tag}_dt3"] = (s.sD, s.sD.v)

        def tdl(t):
            s.w(t, f"{tag}_dt3")
            s.w(t, "psD_free")
            t.transpose(s.psD[0:32, 0:128], s.dl[:, 0:32], s.identf[:, :])
            t.transpose(s.psD[0:32, 128:256], s.dl[:, 32:64],
                        s.identf[:, :]).then_inc(s.tD.h, 1)
            s.tD.v += 1
        s.T(tdl)
        s.ev[f"{tag}_tdl"] = (s.tD, s.tD.v)

        def cdl(sc):
            s.w(sc, f"{tag}_tdl")
            sc.copy(s.dtcT[:, :], s.psD[0:32, 0:128])
            sc.copy(s.lndtT[:, :], s.psD[0:32, 128:256]).then_inc(s.sD.h, 1)
            s.sD.v += 1
        s.C(cdl)
        s.ev[f"{tag}_dtlT"] = (s.sD, s.sD.v)

        def vac(v):
            s.w(v, f"{tag}_dtlT")
            s.w(v, "p0_acol")
            s.vch(v, v.tensor_scalar(s.A_dt[:, :], s.dtcT[:, :], s.A_col[:, 0:1], None, ALU.mult))
            s.vch(v, v.tensor_tensor_scan(s.acs[:, :], s.A_dt[:, :], s.A_dt[:, :], 0.0, ALU.add, ALU.bypass))
            s.vch(v, v.tensor_copy(s.achl[:, 0:128], s.acs[:, :]))
            s.vch(v, v.tensor_tensor(s.achl[:, 128:256], s.acs[:, :], s.achl[:, 0:128], ALU.subtract))
            s.vch(v, v.tensor_tensor(s.pk3[0:32, :], s.lndtT[:, :], s.acs[:, :], ALU.subtract))
            v.tensor_scalar(s.junk[0:HC, 0:128], s.acs[:, :], s.acs[:, 127:128], -1.0,
                            ALU.subtract, ALU.mult).then_inc(s.vD.h, 1)
            s.vD.v += 1
        s.V(vac)
        s.ev[f"{tag}_ac"] = (s.vD, s.vD.v)

        def cex(sc):
            s.w(sc, f"{tag}_ac")
            sc.activation(s.junk[0:HC, 128:256], s.junk[0:HC, 0:128], AF.Exp)
            sc.activation(s.pk3[64:96, :], s.acs[:, :], AF.Exp).then_inc(s.sD.h, 1)
            s.sD.v += 1
        s.C(cex)
        s.ev[f"{tag}_dec"] = (s.sD, s.sD.v)

        def vdd(v):
            s.w(v, f"{tag}_dec")
            v.tensor_tensor(s.pk3[32:64, :], s.dtcT[:, :],
                            s.junk[0:HC, 128:256], ALU.mult).then_inc(s.vD.h, 1)
            s.vD.v += 1
        s.V(vdd)
        s.ev[f"{tag}_ddw"] = (s.vD, s.vD.v)
        s.ev[f"{tag}_dlfree"] = (s.vD, s.vD.v)

        def tp3(t):
            s.w(t, f"{tag}_ddw")
            t.transpose(s.psD[:, 128:224], s.pk3[0:96, :], s.identf[0:96, 0:96]).then_inc(s.tD.h, 1)
            s.tD.v += 1
        s.T(tp3)
        s.ev[f"{tag}_tp3"] = (s.tD, s.tD.v)

        def cp3(sc):
            s.w(sc, f"{tag}_tp3")
            sc.copy(s.packT[:, :], s.psD[:, 128:224]).then_inc(s.sD.h, 1)
            s.sD.v += 1
        s.C(cp3)
        s.ev[f"{tag}_packT"] = (s.sD, s.sD.v)

        def vdt2(v):
            s.w(v, f"{tag}_packT")
            v.tensor_copy(s.ddTb[:, :], s.packT[:, 32:64]).then_inc(s.vD.h, 1)
            s.vD.v += 1
        s.V(vdt2)
        s.ev[f"{tag}_ddTb"] = (s.vD, s.vD.v)

        def tcd(t):
            s.w(t, f"{tag}_packT")
            t.matmul(s.psD[:, 224:256], s.sel127[:, :], s.packT[:, 64:96],
                     start=True, stop=True).then_inc(s.tD.h, 1)
            s.tD.v += 1
        s.T(tcd)
        s.ev[f"{tag}_cdm"] = (s.tD, s.tD.v)

        def ccd(sc):
            s.w(sc, f"{tag}_cdm")
            sc.copy(s.cdrep[:, :], s.psD[:, 224:256]).then_inc(s.sD.h, 1)
            s.sD.v += 1
        s.C(ccd)
        s.ev[f"{tag}_cdrep"] = (s.sD, s.sD.v)

        def tbt(t):
            s.w(t, f"p2_{Si}_bc_done")
            if zz > 0:
                s.w(t, f"{'p2_%d_z%d' % (Si, z-1) if z>0 else 'p2_%d_z%d' % (Si-1, ZS-1)}_btev")
            for g_i in range(2):
                inst = t.transpose(s.psE[:, g_i * 128:(g_i + 1) * 128],
                                   s.BC_sb[:, g_i * 1024 + z * 128: g_i * 1024 + (z + 1) * 128],
                                   s.identb[:, :])
            inst.then_inc(s.tD.h, 1)
            s.tD.v += 1
        s.T(tbt)
        s.ev[f"{tag}_bt"] = (s.tD, s.tD.v)

        def cbt(sc):
            s.w(sc, f"{tag}_bt")
            sc.copy(s.Btok[:, :], s.psE[:, 0:256]).then_inc(s.sD.h, 1)
            s.sD.v += 1
        s.C(cbt)
        s.ev[f"{tag}_btev"] = (s.sD, s.sD.v)

        def vhd(v):
            s.w(v, f"{tag}_hs")
            s.w(v, f"{tag}_ddTb")
            v.tensor_tensor(s.hsdd[:, :].rearrange("p (h q) -> p h q", h=HC),
                            s.hs_tok[:, :].rearrange("p (h q) -> p h q", h=HC),
                            s.ddTb[:, :].unsqueeze(2).broadcast_to((128, HC, 64)),
                            ALU.mult).then_inc(s.vD.h, 1)
            s.vD.v += 1
        s.V(vhd)
        s.ev[f"{tag}_hsdd"] = (s.vD, s.vD.v)

        def tbc(t):
            s.w(t, f"p2_{Si}_bc_done")
            s.w(t, f"{tag}_cdrep")
            for g_i in range(2):
                inst = t.matmul(s.psD[:, 256 + g_i * 128:256 + (g_i + 1) * 128],
                                s.BC_sb[:, g_i * 1024 + z * 128: g_i * 1024 + (z + 1) * 128],
                                s.BC_sb[:, (2 + g_i) * 1024 + z * 128: (2 + g_i) * 1024 + (z + 1) * 128],
                                start=True, stop=True)
            inst.then_inc(s.tD.h, 1)
            s.tD.v += 1
        s.T(tbc)
        s.ev[f"{tag}_bc"] = (s.tD, s.tD.v)

        for g_i in range(2):
            gt = f"{tag}_g{g_i}"

            def tpan(t, g_i=g_i, gt=gt):
                s.w(t, f"{tag}_ac")
                if g_i == 0:
                    key = (f"p2_{Si}_z{z-1}_g1_scan" if z > 0 else
                           (f"p2_{Si-1}_z{ZS-1}_g1_scan" if Si > 0 else "p1b_7_res"))
                    if key in s.ev:
                        s.w(t, key)
                else:
                    s.w(t, f"{tag}_g0_stfree")
                for hh in range(16):
                    hq = g_i * 16 + hh
                    t.matmul(s.psA[:, hh * 128:(hh + 1) * 128],
                             s.selbank[:, hq * 128:(hq + 1) * 128],
                             s.achl[:, 0:128], start=True, stop=False)
                    inst = t.matmul(s.psA[:, hh * 128:(hh + 1) * 128],
                                    s.selbank[:, hq * 128:(hq + 1) * 128],
                                    s.achl[:, 128:256], start=False, stop=True)
                inst.then_inc(s.tD.h, 1)
                s.tD.v += 1
            s.T(tpan)
            s.ev[f"{gt}_pan"] = (s.tD, s.tD.v)

            def vsel(v, g_i=g_i, gt=gt):
                s.w(v, f"{gt}_pan")
                s.w(v, f"{tag}_packT")
                if g_i == 0 and zz > 0:
                    s.w(v, f"p2_{(zz-1)//ZS}_z{(zz-1)%ZS}_sq")
                v.tensor_tensor(
                    s.selt[:, :].rearrange("p (h l) -> p h l", h=16),
                    s.psA[:, 0:2048].rearrange("p (h l) -> p h l", h=16),
                    s.packT[:, g_i * 16:(g_i + 1) * 16].unsqueeze(2).broadcast_to((128, 16, 128)),
                    ALU.add).then_inc(s.vD.h, 1)
                s.vD.v += 1
            s.V(vsel)
            s.ev[f"{gt}_selt"] = (s.vD, s.vD.v)

            def gsel(g, gt=gt):
                s.w(g, f"{gt}_selt")
                g.affine_select(s.selt[:, :], s.selt[:, :],
                                pattern=[[0, 16], [1, 128]], compare_op=ALU.is_ge,
                                fill=s.fillreg, base=0, channel_multiplier=-1).then_inc(s.gD.h, 1)
                s.gD.v += 1
            s.G(gsel)
            s.ev[f"{gt}_sel2"] = (s.gD, s.gD.v)

            def cexp(sc, gt=gt):
                s.w(sc, f"{gt}_sel2")
                sc.activation(s.expsel[:, :], s.selt[:, :], AF.Exp).then_inc(s.sD.h, 1)
                s.sD.v += 1
            s.C(cexp)
            s.ev[f"{gt}_exp"] = (s.sD, s.sD.v)

            def vmp(v, g_i=g_i, gt=gt):
                s.w(v, f"{gt}_exp")
                s.w(v, f"{tag}_bc")
                v.tensor_tensor(
                    s.expsel[:, :].rearrange("p (h l) -> p h l", h=16),
                    s.expsel[:, :].rearrange("p (h l) -> p h l", h=16),
                    s.psD[:, 256 + g_i * 128:256 + (g_i + 1) * 128]
                        .unsqueeze(1).broadcast_to((128, 16, 128)),
                    ALU.mult).then_inc(s.vD.h, 1)
                s.vD.v += 1
            s.V(vmp)
            s.ev[f"{gt}_mp"] = (s.vD, s.vD.v)

            def tyd(t, g_i=g_i, gt=gt):
                s.w(t, f"{gt}_mp")
                s.w(t, f"{tag}_hsdd")
                s.w(t, f"{tag}_btev")
                for hh in range(16):
                    hq = g_i * 16 + hh
                    t.matmul(s.psA[:, hh * 64:(hh + 1) * 64],
                             s.expsel[:, hh * 128:(hh + 1) * 128],
                             s.hs_tok[:, hq * 64:(hq + 1) * 64], start=True, stop=True)
                for hh in range(16):
                    hq = g_i * 16 + hh
                    inst = t.matmul(s.psA[:, 1024 + hh * 64:1024 + (hh + 1) * 64],
                                    s.BC_sb[:, (2 + g_i) * 1024 + z * 128:(2 + g_i) * 1024 + (z + 1) * 128],
                                    s.prevbf[:, hq * 64:(hq + 1) * 64], start=True, stop=True)
                inst.then_inc(s.tD.h, 1)
                s.tD.v += 1
            s.T(tyd)
            s.ev[f"{gt}_yd"] = (s.tD, s.tD.v)

            def vy(v, g_i=g_i, gt=gt):
                s.w(v, f"{gt}_yd")
                s.vch(v, v.tensor_tensor(
                    s.yoffs[:, :].rearrange("p (h q) -> p h q", h=16),
                    s.psA[:, 1024:2048].rearrange("p (h q) -> p h q", h=16),
                    s.packT[:, 64 + g_i * 16:64 + (g_i + 1) * 16]
                        .unsqueeze(2).broadcast_to((128, 16, 64)),
                    ALU.mult))
                s.vch(v, v.tensor_tensor(
                    s.hsD[:, :].rearrange("p (h q) -> p h q", h=16),
                    s.hs_tok[:, g_i * 1024:(g_i + 1) * 1024].rearrange("p (h q) -> p h q", h=16),
                    s.Drep[:, g_i * 16:(g_i + 1) * 16].unsqueeze(2).broadcast_to((128, 16, 64)),
                    ALU.mult))
                s.vch(v, v.tensor_tensor(s.hsD[:, :], s.hsD[:, :], s.psA[:, 0:1024], ALU.add))
                v.tensor_tensor(s.Ysb[:, g_i * 1024:(g_i + 1) * 1024],
                                s.hsD[:, :], s.yoffs[:, :], ALU.add).then_inc(s.vD.h, 1)
                s.vD.v += 1
            s.V(vy)
            s.ev[f"{gt}_y"] = (s.vD, s.vD.v)

            def tst(t, g_i=g_i, gt=gt):
                s.w(t, f"{gt}_y")
                for hh in range(16):
                    hq = g_i * 16 + hh
                    inst = t.matmul(s.psA[:, hh * 64:(hh + 1) * 64],
                                    s.Btok[:, g_i * 128:(g_i + 1) * 128],
                                    s.hsdd[:, hq * 64:(hq + 1) * 64], start=True, stop=True)
                inst.then_inc(s.tD.h, 1)
                s.tD.v += 1
            s.T(tst)
            s.ev[f"{gt}_st"] = (s.tD, s.tD.v)

            def vscan(v, g_i=g_i, gt=gt):
                s.w(v, f"{gt}_st")
                s.w(v, f"{tag}_cdrep")
                sl = slice(g_i * 1024, (g_i + 1) * 1024)
                s.vch(v, v.tensor_tensor(
                    s.hsD[:, 0:1024].rearrange("p (h q) -> p h q", h=16),
                    s.prev[:, sl].rearrange("p (h q) -> p h q", h=16),
                    s.cdrep[:, g_i * 16:(g_i + 1) * 16].unsqueeze(2).broadcast_to((128, 16, 64)),
                    ALU.mult))
                s.vch(v, v.tensor_tensor(s.prev[:, sl], s.hsD[:, 0:1024], s.psA[:, 0:1024], ALU.add))
                v.tensor_copy(s.prevbf[:, sl], s.prev[:, sl]).then_inc(s.vD.h, 1)
                s.vD.v += 1
            s.V(vscan)
            s.ev[f"{gt}_scan"] = (s.vD, s.vD.v)
            s.ev[f"{gt}_stfree"] = (s.vD, s.vD.v)

        s.ev[f"{tag}_hsfree"] = s.ev[f"{tag}_g1_scan"]
        s.ev["psD_free"] = s.ev[f"{tag}_g1_mp"]

        def csg(sc):
            s.w(sc, f"{tag}_gt")
            sc.activation(s.gsil[:, :], s.gatet[:, :], AF.Silu).then_inc(s.sD.h, 1)
            s.sD.v += 1
        s.C(csg)
        s.ev[f"{tag}_gsil"] = (s.sD, s.sD.v)

        def vgp(v):
            s.w(v, f"{tag}_gsil")
            s.w(v, f"{tag}_g1_y")
            v.tensor_tensor(s.Ysb[:, :], s.Ysb[:, :], s.gsil[:, :], ALU.mult).then_inc(s.vD.h, 1)
            s.vD.v += 1
        s.V(vgp)
        s.ev[f"{tag}_gp"] = (s.vD, s.vD.v)

        def csq(sc):
            s.w(sc, f"{tag}_gp")
            sc.activation(s.selt[:, 0:1024], s.Ysb[:, 0:1024], AF.Square,
                          accum_out=s.ssq[:, 0:1])
            sc.activation(s.selt[:, 1024:2048], s.Ysb[:, 1024:2048], AF.Square,
                          accum_out=s.ssq[:, 1:2]).then_inc(s.sD.h, 1)
            s.sD.v += 1
        s.C(csq)
        s.ev[f"{tag}_sq"] = (s.sD, s.sD.v)

        def vrs(v):
            s.w(v, f"{tag}_sq")
            v.tensor_scalar(s.ssq[:, 2:4], s.ssq[:, 0:2], 1.0 / 1024.0, EPS, ALU.mult, ALU.add).then_inc(s.vD.h, 1)
            s.vD.v += 1
        s.V(vrs)
        s.ev[f"{tag}_vv"] = (s.vD, s.vD.v)

        def css(sc):
            s.w(sc, f"{tag}_vv")
            sc.activation(s.ssq[:, 4:6], s.ssq[:, 2:4], AF.Sqrt).then_inc(s.sD.h, 1)
            s.sD.v += 1
        s.C(css)
        s.ev[f"{tag}_sqt"] = (s.sD, s.sD.v)

        def vgo(v):
            s.w(v, f"{tag}_sqt")
            if zz > 0:
                s.w(v, f"p2_{(zz-1)//ZS}_z{(zz-1)%ZS}_gw")
            s.vch(v, v.reciprocal(s.ssq[:, 6:8], s.ssq[:, 4:6]))
            for g_i in range(2):
                inst = v.tensor_scalar(s.gatet[:, g_i * 1024:(g_i + 1) * 1024],
                                       s.Ysb[:, g_i * 1024:(g_i + 1) * 1024],
                                       s.ssq[:, 6 + g_i:7 + g_i], None, ALU.mult)
            inst.then_inc(s.vD.h, 1)
            s.vD.v += 1
        s.V(vgo)
        s.ev[f"{tag}_go"] = (s.vD, s.vD.v)

        def ygw(y):
            s.w(y, f"{tag}_go")
            y.dma_start(s.gin.ap()[zz * 128:(zz + 1) * 128, :], s.gatet[:, :]).then_inc(s.qGW.h, 16)
            s.qGW.v += 16
        s.Y(ygw)
        s.ev[f"{tag}_gw"] = (s.qGW, s.qGW.v)

    def p3a(self):
        s = self
        for half in range(2):
            def ystage(y, half=half):
                s.w(y, "ag2")
                with y.register(f"toff{half}") as reg:
                    y.reg_load(reg, s.meta_s[0:1, 0:1])
                    toff = y.snap(reg)
                    for rr in range(2):
                        r = 2 * half + rr
                        y.dma_start(
                            s.gq.ap()[(2 * half + rr) * TOK:(2 * half + rr + 1) * TOK, :],
                            s.gout4.ap()[bass.ds(toff + r * S, TOK), :],
                        ).then_inc(s.qGQ.h, 16)
                        s.qGQ.v += 16
            s.Y(ystage)
            s.ev[f"p3_gq{half}"] = (s.qGQ, s.qGQ.v)

            def ytr(y, half=half):
                s.w(y, f"p3_gq{half}")
                if half == 1:
                    s.w(y, f"p3_op0_{16*NT-1}")
                for rr in range(2):
                    for cb in range(16):
                        ic = rr * 16 + cb
                        y.dma_start(
                            s.goT[:, ic * TOK:(ic + 1) * TOK],
                            s.gq.ap()[(2 * half + rr) * TOK:(2 * half + rr + 1) * TOK,
                                      cb * 128:(cb + 1) * 128],
                            transpose=True).then_inc(s.qGO.h, 16)
                        s.qGO.v += 16
            s.Y(ytr)
            s.ev[f"p3_goT{half}"] = (s.qGO, s.qGO.v)

            def vnw(v, half=half):
                s.w(v, f"p3_goT{half}")
                for ic in range(32):
                    inst = v.tensor_scalar(s.goT[:, ic * TOK:(ic + 1) * TOK],
                                           s.goT[:, ic * TOK:(ic + 1) * TOK],
                                           s.nw_s[:, 32 * half + ic:32 * half + ic + 1],
                                           None, ALU.mult)
                inst.then_inc(s.vD.h, 1)
                s.vD.v += 1
            s.V(vnw)
            s.ev[f"p3_nw{half}"] = (s.vD, s.vD.v)

            for nb in range(16):
                def gwo(g, nb=nb, half=half):
                    hnb = half * 16 + nb
                    if hnb < 2:
                        s.w(g, "ag2")
                    else:
                        s.w(g, f"p3_opp{(hnb-2)//16}_{((hnb-2)%16)*NT + NT - 1}")
                    q = s.qWO[nb % 2]
                    g.dma_start(
                        s.wob[nb % 2][:, :].rearrange("p (c w) -> p c w", c=32),
                        s.wo[half * 4096:(half + 1) * 4096,
                             nb * 256:(nb + 1) * 256].rearrange("(c p) w -> p c w", p=128),
                    ).then_inc(q.h, 16)
                    q.v += 16
                s.G(gwo)
                q2 = s.qWO[nb % 2]
                s.ev[f"p3_wo{half}_{nb}"] = (q2, q2.v)

                for t_i in range(NT):
                    idx = (half * 16 + nb) * NT + t_i
                    reg = (idx % 2) * 512

                    def yx1(y, nb=nb, t_i=t_i, idx=idx, half=half):
                        if idx >= 2:
                            s.w(y, f"p3_ev{idx-2}")
                        if half == 0:
                            src = s.x1d[t_i * 128:(t_i + 1) * 128, nb * 256:(nb + 1) * 256]
                        else:
                            src = s.x2d[t_i * 128:(t_i + 1) * 128, nb * 256:(nb + 1) * 256]
                        if half == 1:
                            s.w(y, f"p3_x2w{(0 * 16 + nb) * NT + t_i}")
                        q = s.qXS[idx % 2]
                        y.dma_start(s.x1s[idx % 2][:, :], src).then_inc(q.h, 16)
                        q.v += 16
                    s.Y(yx1)
                    s.ev[f"p3_x1_{idx}"] = (s.qXS[idx % 2], s.qXS[idx % 2].v)

                    def top(t, nb=nb, t_i=t_i, idx=idx, reg=reg, half=half):
                        s.w(t, f"p3_nw{half}")
                        s.w(t, f"p3_wo{half}_{nb}")
                        if idx >= 2:
                            s.w(t, f"p3_ev{idx-2}")
                        for k in range(32):
                            inst = t.matmul(s.psC[:, reg:reg + 256],
                                            s.goT[:, k * TOK + t_i * 128: k * TOK + (t_i + 1) * 128],
                                            s.wob[nb % 2][:, k * 256:(k + 1) * 256],
                                            start=(k == 0), stop=(k == 31))
                        inst.then_inc(s.tD.h, 1)
                        s.tD.v += 1
                    s.T(top)
                    s.ev[f"p3_opp{half}_{nb * NT + t_i}"] = (s.tD, s.tD.v)
                    s.ev[f"p3_op{half}_{nb * NT + t_i}"] = (s.tD, s.tD.v)

                    def vev(v, nb=nb, t_i=t_i, idx=idx, reg=reg, half=half):
                        s.w(v, f"p3_op{half}_{nb * NT + t_i}")
                        s.w(v, f"p3_x1_{idx}")
                        if idx >= 2:
                            s.w(v, f"p3_x2w{idx-2}")
                        v.scalar_tensor_tensor(s.ost[idx % 2][:, :], s.psC[:, reg:reg + 256],
                                               1.0, s.x1s[idx % 2][:, :],
                                               ALU.mult, ALU.add).then_inc(s.vD.h, 1)
                        s.vD.v += 1
                    s.V(vev)
                    s.ev[f"p3_ev{idx}"] = (s.vD, s.vD.v)

                    def yx2(y, nb=nb, t_i=t_i, idx=idx):
                        s.w(y, f"p3_ev{idx}")
                        q = s.qOW[idx % 2]
                        y.dma_start(s.x2d[t_i * 128:(t_i + 1) * 128, nb * 256:(nb + 1) * 256],
                                    s.ost[idx % 2][:, :]).then_inc(q.h, 16)
                        q.v += 16
                    s.Y(yx2)
                    s.ev[f"p3_x2w{idx}"] = (s.qOW[idx % 2], s.qOW[idx % 2].v)
        s.ev["p3_x2_done0"] = (s.qOW[0], s.qOW[0].v)
        s.ev["p3_x2_done1"] = (s.qOW[1], s.qOW[1].v)

    def p3bc(self):
        s = self

        for t_i in range(NT):
            buf = s.X0[t_i % 2]

            def yld(y, t_i=t_i, buf=buf):
                s.w(y, "p3_x2_done0")
                s.w(y, "p3_x2_done1")
                if t_i >= 2:
                    s.w(y, f"p3b_{t_i-2}_out")
                q = s.qX0[t_i % 2]
                y.dma_start(buf[:, :], s.x2d[t_i * 128:(t_i + 1) * 128, :]).then_inc(q.h, 16)
                q.v += 16
            s.Y(yld)
            s.ev[f"p3b_{t_i}_in"] = (s.qX0[t_i % 2], s.qX0[t_i % 2].v)
            extra = f"p3b_{t_i-1}_wr" if t_i >= 1 else None
            s.ln(f"p3b_{t_i}", buf, s.XNB, f"p3b_{t_i}_in", extra_wait=extra)

            def ywr(y, t_i=t_i):
                s.w(y, f"p3b_{t_i}_out")
                y.dma_start(s.xn3d[t_i * 128:(t_i + 1) * 128, :], s.XNB[:, :]).then_inc(s.qNW.h, 16)
                s.qNW.v += 16
            s.Y(ywr)
            s.ev[f"p3b_{t_i}_wr"] = (s.qNW, s.qNW.v)

        def ctr3(sc):
            s.w(sc, f"p3b_{NT-1}_wr")
            for cb in range(32):
                sc.dma_start(s.xnT[:, cb * TOK:(cb + 1) * TOK],
                             s.xn3d[:, cb * 128:(cb + 1) * 128],
                             transpose=True).then_inc(s.qC.h, 16)
                s.qC.v += 16
        s.C(ctr3)
        s.ev["p3c_xnT"] = (s.qC, s.qC.v)

        for t_i in range(NT):
            buf = s.X0[t_i % 2]

            def yld(y, t_i=t_i, buf=buf):
                if t_i >= 2:
                    s.w(y, f"p3c_{t_i-2}_res")
                q = s.qX0[t_i % 2]
                y.dma_start(buf[:, :], s.x2d[t_i * 128:(t_i + 1) * 128, :]).then_inc(q.h, 16)
                q.v += 16
            s.Y(yld)
            s.ev[f"p3c_{t_i}_xin"] = (s.qX0[t_i % 2], s.qX0[t_i % 2].v)

            prev = f"p3c_{t_i-1}" if t_i >= 1 else None
            xwfree = [f"p3c_{t_i-1}_ywr"] if t_i >= 1 else None
            s.ffn_tile(f"p3c_{t_i}", t_i, buf, f"p3c_{t_i}_xin", "p3c_xnT",
                       prev_tag=prev, xw_free_ev=xwfree)

            def ywy(y, t_i=t_i):
                s.w(y, f"p3c_{t_i}_res")
                y.dma_start(s.y_out[t_i * 128:(t_i + 1) * 128, :], s.XW[:, :]).then_inc(s.qY.h, 16)
                s.qY.v += 16
            s.Y(ywy)
            s.ev[f"p3c_{t_i}_ywr"] = (s.qY, s.qY.v)

        def yfin(y):
            s.w(y, f"p3c_{NT-1}_ywr")
        s.Y(yfin)


_CACHE = {}


def _build():
    if "nc" not in _CACHE:
        _CACHE["nc"] = KB().build()
    return _CACHE["nc"]


def kernel(L0_attn_x, L0_attn_in_proj_w, L0_attn_conv_w, L0_attn_conv_b,
           L0_attn_dt_bias, L0_attn_A_log, L0_attn_D, L0_attn_norm_w,
           L0_attn_out_proj_w, L0_ffn_route_w, L0_ffn_A, L0_ffn_Bw):
    x = np.asarray(L0_attn_x, dtype=np.float32)
    ipw = np.asarray(L0_attn_in_proj_w, dtype=np.float32)
    cwv = np.asarray(L0_attn_conv_w, dtype=np.float32)[:, 0, :]
    cbv = np.asarray(L0_attn_conv_b, dtype=np.float32)
    dtbv = np.asarray(L0_attn_dt_bias, dtype=np.float32)
    alogv = np.asarray(L0_attn_A_log, dtype=np.float32)
    dv = np.asarray(L0_attn_D, dtype=np.float32)
    nwv = np.asarray(L0_attn_norm_w, dtype=np.float32)
    opw = np.asarray(L0_attn_out_proj_w, dtype=np.float32)
    rw = np.asarray(L0_ffn_route_w, dtype=np.float32)
    Av = np.asarray(L0_ffn_A, dtype=np.float32)
    Bwv = np.asarray(L0_ffn_Bw, dtype=np.float32)

    wo_t = np.ascontiguousarray(opw.T)
    fw_t = np.ascontiguousarray(
        np.concatenate([Av.reshape(128, HID), rw], axis=0).T)
    bw_t = np.ascontiguousarray(Bwv.transpose(0, 2, 1).reshape(128, HID))
    nw_t = np.ascontiguousarray(nwv.reshape(64, 128).T)

    in_maps = []
    for q in range(8):
        b, m = q // 4, q % 4
        rows = np.r_[2048 * m:2048 * m + 2048,
                     8192 + 2048 * m:8192 + 2048 * m + 2048,
                     16384 + 256 * m:16384 + 256 * m + 256,
                     17408 + 256 * m:17408 + 256 * m + 256,
                     18432 + 32 * m:18432 + 32 * m + 32]
        wsl = np.ascontiguousarray(ipw[rows, :].T)
        crow = np.r_[2048 * m:2048 * m + 2048,
                     8192 + 256 * m:8192 + 256 * m + 256,
                     9216 + 256 * m:9216 + 256 * m + 256]
        cws = np.ascontiguousarray(
            cwv[crow, :].reshape(NCB, 128, 4).transpose(1, 0, 2).reshape(128, NCB * 4))
        cbs = np.ascontiguousarray(cbv[crow].reshape(NCB, 128).T)
        in_maps.append({
            "x": np.ascontiguousarray(x[b, 1024 * m:1024 * (m + 1), :]),
            "wsl": wsl,
            "convw": cws,
            "convb": cbs,
            "dtb": np.ascontiguousarray(dtbv[32 * m:32 * m + 32].reshape(1, 32)),
            "alog": np.ascontiguousarray(alogv[32 * m:32 * m + 32].reshape(32, 1)),
            "dvec": np.ascontiguousarray(dv[32 * m:32 * m + 32].reshape(1, 32)),
            "nw": nw_t,
            "wo": wo_t,
            "fw": fw_t,
            "bwf": bw_t,
            "meta": np.array([[1024 * m, 0]], dtype=np.int32),
        })

    nc = _build()
    res = run_bass_kernel_spmd(nc, in_maps, core_ids=list(range(8)))
    out = np.empty((B, S, HID), dtype=np.float32)
    for q in range(8):
        b, m = q // 4, q % 4
        out[b, 1024 * m:1024 * (m + 1), :] = res.results[q]["y"]
    return out
